# version 05
# baseline (speedup 1.0000x reference)
import numpy as np
import ml_dtypes

import concourse.bass as bass
import concourse.mybir as mybir
from concourse.bass_utils import run_bass_kernel_spmd

F32 = mybir.dt.float32
BF16 = mybir.dt.bfloat16
AF = mybir.ActivationFunctionType
ALU = mybir.AluOpType
AX = mybir.AxisListType

D = 2048
T = 4096
B = 2
H = 16
HKV = 4
NPOOL = 48
NEG = -1.0e9
FFH = 1408
NE = 16
NSLOT = 8
NXP = 17
VIS = [[0]] * 6 + [[0, 1, 2, 3, 4], list(range(9))]
SQS = float(1.0 / np.sqrt(32.0))
DMA_CN = ("dx", "dxo", "dwa", "dwf", "dmk", "dcs", "dou", "dav")


class _Dummy:
    def then_inc(self, *a, **k):
        return self


_DUM = _Dummy()


class Proxy:
    def __init__(self, eng):
        self.e = eng


class Res:
    __slots__ = ("name", "w", "readers")

    def __init__(self, name):
        self.name = name
        self.w = {}
        self.readers = []


class IO:
    def __init__(self, sems, live_name, eng):
        self.cnt = {k: 0 for k in sems}
        self.sems = sems
        self.waited = {k: 0 for k in sems}
        for nm in ["t", "v", "s", "g", "y"]:
            setattr(self, nm, Proxy(eng if nm == live_name else None))
        self.pmap = {}
        self.pmap[id(self.t)] = "pe"
        self.pmap[id(self.v)] = "dve"
        self.pmap[id(self.s)] = "act"
        self.pmap[id(self.g)] = "pool"
        self.pmap[id(self.y)] = "spq"

    def n(self, cname):
        return self.cnt[cname]

    def wait(self, proxy, cname, val):
        if val <= 0:
            return
        if proxy.e is not None:
            if self.waited[cname] >= val:
                return
            proxy.e.wait_ge(self.sems[cname], val)
            self.waited[cname] = val

    def emit(self, proxy, fns, reads=(), writes=(), inc=1, cname=None):
        if cname is None:
            cname = self.pmap[id(proxy)]
        for r in reads:
            for cn, vv in r.w.items():
                self.wait(proxy, cn, vv)
        for r in writes:
            for cn, vv in r.w.items():
                self.wait(proxy, cn, vv)
            for rd in r.readers:
                self.wait(proxy, *rd)
        dma = cname in DMA_CN
        if dma:
            assert inc == 16 * len(fns), (cname, inc, len(fns))
        last = _DUM
        if proxy.e is not None:
            for f in fns:
                last = f()
                if dma and last is not _DUM:
                    last.then_inc(self.sems[cname], 16)
        if (last is not _DUM) and not dma:
            last.then_inc(self.sems[cname], inc)
        self.cnt[cname] += inc
        mark = (cname, self.cnt[cname])
        for r in reads:
            r.readers.append(mark)
        for r in writes:
            r.w = {cname: self.cnt[cname]}
            r.readers = []
        return mark


def build_program():
    nc = bass.Bass("TRN2", target_bir_lowering=False, debug=False)

    def din(name, shape, dtype=F32):
        return nc.dram_tensor(name, list(shape), dtype, kind="ExternalInput").ap()

    I = {}
    I["xb"] = din("xb", [T, D])
    I["x_own"] = din("x_own", [NSLOT * 128, D])
    I["wks"] = din("wks", [D, 128])
    I["wkg"] = din("wkg", [D, 128])
    I["wvL"] = din("wvL", [4, 128, 16, 64])
    I["wqsL"] = din("wqsL", [4, 128, 16, 128])
    I["wqgL"] = din("wqgL", [4, 128, 16, 128])
    I["wo"] = din("wo", [1024, D])
    I["gw"] = din("gw", [NE, D])
    I["w1L"] = din("w1L", [NXP, 11, 128, 16, 128])
    I["w2A"] = din("w2A", [NXP, FFH, D])
    I["lsx"] = din("lsx", [128, 4])
    I["csc"] = din("csc", [128, T], BF16)
    I["css"] = din("css", [128, T], BF16)
    I["cso_c"] = din("cso_c", [128, NSLOT * 128], BF16)
    I["cso_s"] = din("cso_s", [128, NSLOT * 128], BF16)
    I["negid"] = din("negid", [128, 128], BF16)
    I["pswap"] = din("pswap", [128, 128], BF16)
    I["mstd"] = din("mstd", [NSLOT, 9, 128, 128], BF16)
    I["mtt"] = din("mtt", [NSLOT, 9, 128, 128], BF16)
    I["degA"] = din("degA", [128, 128], BF16)
    I["degB"] = din("degB", [128, 128], BF16)
    y_out = nc.dram_tensor("y", [NSLOT * 128, D], F32, kind="ExternalOutput").ap()
    av_dram = nc.dram_tensor("av_scr", [NSLOT, 128, 1024], BF16,
                             kind="Internal").ap()

    from contextlib import ExitStack
    ctx = ExitStack()
    sball = lambda name, shape, dt: ctx.enter_context(
        nc.sbuf_tensor(name, list(shape), dt))

    y_acc = sball("y_acc", [128, NSLOT * D], F32)
    xn_own = sball("xn_own", [128, 16 * 1024], BF16)
    comb = sball("comb", [128, NSLOT * 16], F32)
    scol = sball("scol", [128, 4], F32)
    ones_col = sball("ones_col", [128, 1], BF16)
    negid_sb = sball("negid_sb", [128, 128], BF16)
    pswap_sb = sball("pswap_sb", [128, 128], BF16)
    lsx_sb = sball("lsx_sb", [128, 4], F32)
    rsc = sball("rsc", [128, 136], F32)
    pscr = None
    lnscr2 = [sball(f"lnscr{i}", [128, 32], F32) for i in range(2)]
    bn6 = [tt[:, 0:24] for tt in lnscr2]
    bn2 = [tt[:, 24:26] for tt in lnscr2]
    bn2_mean = [tt[:, 24:25] for tt in lnscr2]
    bn2_var = [tt[:, 25:26] for tt in lnscr2]
    lnv = [tt[:, 26:27] for tt in lnscr2]
    rstd = [tt[:, 27:28] for tt in lnscr2]
    epsc = lnscr2[0][:, 29:30]
    kaug = [sball(f"kaug{g}", [65, 9 * 128], BF16) for g in range(HKV)]
    v_sb = [sball(f"v{c}", [128, 256], BF16) for c in range(9)]

    OV = 79872
    OVSZ = 17152
    ARENA = OV + OVSZ
    arena = sball("arena", [128, ARENA], mybir.dt.uint8)
    arena_base = nc.lookup_mloc(arena).addr

    def at(name, shape, dtype, off):
        nbytes = int(np.prod(shape[1:])) * mybir.dt.size(dtype)
        assert off + nbytes <= ARENA, f"{name}: {off}+{nbytes} > {ARENA}"
        return nc.alloc_sbuf_tensor_at(name, list(shape), dtype,
                                       offset=arena_base + off)

    w2_sb = at("w2_sb", [128, 11 * D], BF16, 0)
    silu_sb = at("silu_sb", [128, 11 * 1024], BF16, 45056)
    w1r = [at(f"w1r{i}", [128, 2048], BF16, 67584 + 4096 * i) for i in range(3)]
    ost = [at(f"ost{i}", [128, 2048], BF16, 45056 + 4096 * i) for i in range(2)]
    wo_sb = at("wo_sb", [128, 8 * D], BF16, 0)
    avt = [at(f"avt{i}", [128, 1024], BF16, 45056 + 2048 * i) for i in range(2)]

    wkg_sb = at("wkg_sb", [128, 16 * 128], BF16, OV + 0)
    xnfm = at("xnfm", [128, 2048], BF16, OV + 4096)
    wks_sb = at("wks_sb", [128, 16 * 128], BF16, OV + 8192)
    wvq_sb = at("wvq_sb", [128, 16 * 64], BF16, OV + 12288)
    pxn = at("pxn", [128, 16 * NPOOL], BF16, OV + 14336)
    kgraw = at("kgraw", [128, 128], BF16, OV + 15872)
    kgsw = at("kgsw", [128, 128], BF16, OV + 16128)
    csb = at("csb", [128, 128], BF16, OV + 16384)
    kgp = at("kgp", [128, NPOOL], F32, OV + 16640)
    gw_sb = at("gw_sb", [16, D], BF16, OV + 0)
    gwt = at("gwt", [128, 256], BF16, OV + 12288)
    wqb = at("wqb", [128, 2048], BF16, OV + 0)
    qslot = at("qslot", [65, 8 * 128], BF16, OV + 4096)
    av_acc = at("av_acc", [64, 8 * 128], F32, OV + 6144)
    av_st = at("av_st", [128, 1024], BF16, OV + 10240)
    expT = at("expT", [128, 8 * 128], BF16, OV + 12288)
    mcst = at("mcst", [128, 8 * 9], F32, OV + 14336)
    negm = at("negm", [128, 8 * 9], F32, OV + 14624)
    dcst = at("dcst", [128, 8 * 9], F32, OV + 14912)
    edst = at("edst", [128, 8 * 9], F32, OV + 15200)
    mall = at("mall", [128, 8], F32, OV + 15488)
    dall = at("dall", [128, 8], F32, OV + 15520)
    mprime = at("mprime", [128, 8], F32, OV + 15552)
    mprbf = at("mprbf", [128, 8], BF16, OV + 15584)
    mk_r = [at(f"mkr{i}", [128, 128], BF16, OV + 15616 + 256 * i)
            for i in range(2)]
    cso_sl = at("cso_sl", [128, 256], BF16, OV + 16128)
    qg1 = at("qg1", [128, 128], BF16, OV + 16640)
    qg2 = at("qg2", [128, 128], BF16, OV + 16896)

    sgate = rsc[:, 0:16]
    gm1 = rsc[:, 16:20]
    grm = rsc[:, 20:24]
    gs4 = rsc[:, 24:28]
    geq = rsc[:, 28:44]
    gtmp = rsc[:, 44:60]
    gmask = rsc[:, 60:64]
    emask = rsc[:, 64:80]
    gmsk16 = rsc[:, 80:96]
    t1c = rsc[:, 96:97]
    rtc = rsc[:, 97:98]
    dinv = rsc[:, 98:99]
    pscr = rsc[:, 104:136]

    ps_all = ctx.enter_context(nc.psum_tensor("ps_all", [128, 4096], F32))
    ps1 = [ps_all[:, 0:1024], ps_all[:, 1024:2048]]
    ps2 = ps_all[:, 2048:4096]

    sem_names = list(DMA_CN) + ["pe", "dve", "act", "pool", "spq"]
    sems = {nm: ctx.enter_context(nc.semaphore(nm)) for nm in sem_names}
    blk = ctx.enter_context(nc.Block())

    RNAMES = (["consts", "scol", "bnst0", "bnst1", "xn_own", "comb",
               "gw_sb", "gwt", "route",
               "wks", "wkg", "wvq", "w2", "silu", "wo",
               "pxn", "kgp", "kgraw", "kgsw", "csb",
               "qslot", "qgbuf", "cso", "stats", "mpr", "avac", "avst",
               "expT0", "wqb", "xnf0", "xnf1", "ost0", "ost1", "pscr",
               "b0", "b1", "b2", "b3", "b4", "b5", "b6", "b7"]
              + [f"w1r{i}" for i in range(3)]
              + [f"mk{i}" for i in range(2)]
              + [f"avt{i}" for i in range(2)]
              + [f"kaug{g}" for g in range(HKV)]
              + [f"v{c}" for c in range(9)]
              + [f"yac{s}" for s in range(NSLOT)])
    PAIRB = [["b0", "b1"], ["b2", "b3"]]
    MM2B = [["b4", "b5"], ["b6", "b7"]]

    def program(io: IO):
        t, v, s, g, y = io.t, io.v, io.s, io.g, io.y
        E = io.emit
        R = {nm: Res(nm) for nm in RNAMES}
        KA = [R[f"kaug{gg}"] for gg in range(HKV)]
        pair_tog = {"i": 0}

        def pick_pair():
            i = pair_tog["i"]
            pair_tog["i"] ^= 1
            return ps1[i], [R[b] for b in PAIRB[i]]

        def barrier(fence_res, assign_res):
            mark = E(v, [lambda: v.e.memset(rsc[:, 100:101], 0.0)],
                     writes=list(fence_res))
            for r in assign_res:
                r.w = {mark[0]: mark[1]}
                r.readers = []

        E(y, [lambda: y.e.dma_start(out=pswap_sb[:, :], in_=I["pswap"]),
              lambda: y.e.dma_start(out=negid_sb[:, :], in_=I["negid"]),
              lambda: y.e.dma_start(out=lsx_sb[:, :], in_=I["lsx"])],
          writes=[R["consts"]], inc=48, cname="dmk")
        E(v, [lambda: v.e.memset(ones_col[:, :], 1.0)], writes=[R["consts"]])
        E(v, [lambda: v.e.memset(epsc, 1.0e-5)],
          writes=[R["bnst0"], R["bnst1"]])
        for gg in range(HKV):
            E(v, [lambda gg=gg: v.e.memset(kaug[gg][64:65, :], 1.0),
                  lambda gg=gg: v.e.memset(kaug[gg][0:64, 48:128], 0.0)],
              writes=[KA[gg]])
        E(v, [lambda: v.e.memset(v_sb[0][:, :], 0.0)], writes=[R["v0"]])
        E(s, [lambda: s.e.activation(scol[:, :], lsx_sb[:, :], AF.Exp)],
          reads=[R["consts"]], writes=[R["scol"]])
        E(v, [lambda: v.e.tensor_scalar(scol[:, :], scol[:, :], SQS, None,
                                        ALU.mult)],
          reads=[R["scol"]], writes=[R["scol"]])
        for sl in range(NSLOT):
            E(y, [lambda sl=sl: y.e.dma_start(
                    out=y_acc[:, sl * D:(sl + 1) * D],
                    in_=I["x_own"][sl * 128:(sl + 1) * 128, :])],
              writes=[R[f"yac{sl}"]], inc=16, cname="dxo")

        def ln_tile(src_d, row0, dsem, buf, rh, li, dst_own_sl=None):
            E(g, [lambda: g.e.dma_start(out=buf[:, :],
                                        in_=src_d[row0:row0 + 128, :])],
              writes=rh, inc=16, cname=dsem)
            for bc in range(4):
                E(v, [lambda bc=bc: v.e.bn_stats(
                        bn6[li][:, bc * 6:(bc + 1) * 6],
                        buf[:, bc * 512:(bc + 1) * 512])],
                  reads=[rh[bc // 2]], writes=[R[f"bnst{li}"]])
            E(v, [lambda: v.e.bn_aggr(bn2[li], bn6[li])],
              reads=[R[f"bnst{li}"]], writes=[R[f"bnst{li}"]])
            E(s, [lambda: s.e.activation(lnv[li], bn2_var[li], AF.Ln,
                                         bias=epsc)],
              reads=[R[f"bnst{li}"]], writes=[R[f"bnst{li}"]])
            E(s, [lambda: s.e.activation(rstd[li], lnv[li], AF.Exp,
                                         scale=-0.5)],
              reads=[R[f"bnst{li}"]], writes=[R[f"bnst{li}"]])
            yield
            for hf in range(2):
                E(v, [lambda hf=hf: v.e.tensor_scalar(
                        buf[:, hf * 1024:(hf + 1) * 1024],
                        buf[:, hf * 1024:(hf + 1) * 1024],
                        bn2_mean[li], rstd[li],
                        ALU.subtract, ALU.mult)],
                  reads=[rh[hf], R[f"bnst{li}"]], writes=[rh[hf]])
                pp, pb = pick_pair()
                E(t, [(lambda c=c, pp=pp, hf=hf: t.e.matmul(
                        pp[:, c * 128:(c + 1) * 128],
                        buf[:, hf * 1024 + c * 128:hf * 1024 + (c + 1) * 128],
                        negid_sb[:, :], start=True, stop=True))
                      for c in range(8)],
                  reads=[rh[hf], R["consts"]], writes=pb)
                if dst_own_sl is None:
                    E(s, [lambda hf=hf, pp=pp: s.e.activation(
                            buf[:, hf * 1024:(hf + 1) * 1024], pp[:, :],
                            AF.Copy, scale=-1.0)],
                      reads=pb, writes=[rh[hf]])
                else:
                    xo = xn_own[:, :].rearrange("p (kc tk) -> p kc tk",
                                                tk=1024)
                    E(s, [lambda hf=hf, pp=pp, sl=dst_own_sl: s.e.activation(
                            xo[:, hf * 8:(hf + 1) * 8,
                               sl * 128:(sl + 1) * 128],
                            pp[:, :].rearrange("p (c k) -> p c k", k=128),
                            AF.Copy, scale=-1.0)],
                      reads=pb, writes=[R["xn_own"]])
                if hf == 0:
                    yield

        own_gens = [ln_tile(I["x_own"], sl * 128, "dxo", ost[sl % 2],
                            [R[f"ost{sl % 2}"], R[f"ost{sl % 2}"]],
                            sl % 2, sl)
                    for sl in range(NSLOT)]
        for sl in range(0, NSLOT, 2):
            for gx in (own_gens[sl], own_gens[sl + 1]):
                next(gx)
            for st in range(2):
                for gx in (own_gens[sl], own_gens[sl + 1]):
                    try:
                        next(gx)
                    except StopIteration:
                        pass
        barrier([R["ost0"], R["ost1"]], [R["silu"]])

        def side_chain():
            E(g, [lambda: g.e.dma_start(out=gw_sb[:, :], in_=I["gw"])],
              writes=[R["gw_sb"]], inc=16, cname="dmk")
            pp, pb = pick_pair()
            psA, psB = pp[:, 0:512], pp[:, 512:1024]
            E(t, [(lambda kc=kc: t.e.matmul(
                    (psA if kc < 8 else psB)[:, (kc % 8) * 16:(kc % 8) * 16 + 16],
                    gw_sb[:, kc * 128:(kc + 1) * 128], negid_sb[0:16, 0:16],
                    start=True, stop=True)) for kc in range(16)],
              reads=[R["gw_sb"], R["consts"]], writes=pb)
            E(s, [lambda: s.e.activation(gwt[:, 0:128], psA[:, 0:128], AF.Copy,
                                         scale=-1.0),
                  lambda: s.e.activation(gwt[:, 128:256], psB[:, 0:128],
                                         AF.Copy, scale=-1.0)],
              reads=pb, writes=[R["gwt"]])
            yield
            E(v, [lambda: v.e.memset(sgate[:, :], 0.0)], writes=[R["route"]])
            for sl in range(NSLOT):
                pp, pb = pick_pair()
                psQ = pp[:, 0:16]
                E(t, [(lambda kc=kc, sl=sl, psQ=psQ: t.e.matmul(
                        psQ[:, :],
                        xn_own[:, kc * 1024 + sl * 128: kc * 1024 + sl * 128 + 128],
                        gwt[:, kc * 16:(kc + 1) * 16],
                        start=(kc == 0), stop=(kc == 15))) for kc in range(16)],
                  reads=[R["xn_own"], R["gwt"]], writes=pb)
                E(s, [lambda psQ=psQ: s.e.activation(sgate[:, :], psQ[:, :],
                                                     AF.Sigmoid)],
                  reads=pb, writes=[R["route"]])
                rt = R["route"]

                def V1(f):
                    E(v, [f], reads=[rt], writes=[rt])
                for gg in range(4):
                    V1(lambda gg=gg: v.e.tensor_reduce(
                        gm1[:, gg:gg + 1], sgate[:, 4 * gg:4 * gg + 4], AX.X,
                        ALU.max))
                for gg in range(4):
                    V1(lambda gg=gg: v.e.tensor_scalar(
                        geq[:, 4 * gg:4 * gg + 4], sgate[:, 4 * gg:4 * gg + 4],
                        gm1[:, gg:gg + 1], None, ALU.is_equal))
                    V1(lambda gg=gg: v.e.scalar_tensor_tensor(
                        gtmp[:, 4 * gg:4 * gg + 4], geq[:, 4 * gg:4 * gg + 4],
                        1.0e30, sgate[:, 4 * gg:4 * gg + 4], ALU.mult,
                        ALU.subtract))
                    V1(lambda gg=gg: v.e.tensor_reduce(
                        grm[:, gg:gg + 1], gtmp[:, 4 * gg:4 * gg + 4], AX.X,
                        ALU.min))
                V1(lambda: v.e.tensor_tensor(gs4[:, :], gm1[:, :], grm[:, :],
                                             ALU.subtract))
                V1(lambda: v.e.tensor_reduce(t1c[:, :], gs4[:, :], AX.X,
                                             ALU.max))
                V1(lambda: v.e.tensor_scalar(geq[:, 0:4], gs4[:, :],
                                             t1c[:, 0:1], None, ALU.is_equal))
                V1(lambda: v.e.scalar_tensor_tensor(gtmp[:, 0:4], geq[:, 0:4],
                                                    1.0e30, gs4[:, :],
                                                    ALU.mult, ALU.subtract))
                V1(lambda: v.e.tensor_reduce(rtc[:, :], gtmp[:, 0:4], AX.X,
                                             ALU.min))
                V1(lambda: v.e.tensor_scalar(gmask[:, :], gs4[:, :], -1.0,
                                             rtc[:, 0:1], ALU.mult, ALU.is_le))
                V1(lambda: v.e.tensor_copy(
                    emask[:, :].rearrange("p (a b) -> p a b", b=4),
                    gmask[:, :].rearrange("p (a b) -> p a b", b=1)
                    .broadcast_to([128, 4, 4])))
                V1(lambda: v.e.tensor_tensor(gmsk16[:, :], sgate[:, :],
                                             emask[:, :], ALU.mult))
                V1(lambda: v.e.tensor_reduce(t1c[:, :], gmsk16[:, :], AX.X,
                                             ALU.max))
                V1(lambda: v.e.tensor_scalar(geq[:, :], gmsk16[:, :],
                                             t1c[:, 0:1], None, ALU.is_equal))
                V1(lambda: v.e.scalar_tensor_tensor(gtmp[:, :], geq[:, :],
                                                    1.0e30, gmsk16[:, :],
                                                    ALU.mult, ALU.subtract))
                V1(lambda: v.e.tensor_reduce(rtc[:, :], gtmp[:, :], AX.X,
                                             ALU.min))
                V1(lambda: v.e.tensor_scalar(gtmp[:, :], gtmp[:, :],
                                             rtc[:, 0:1], None, ALU.is_equal))
                V1(lambda: v.e.tensor_tensor(geq[:, :], geq[:, :], gtmp[:, :],
                                             ALU.add))
                V1(lambda: v.e.tensor_tensor(gtmp[:, :], gmsk16[:, :],
                                             geq[:, :], ALU.mult))
                V1(lambda: v.e.tensor_tensor(t1c[:, :], t1c[:, :],
                                             rtc[:, 0:1], ALU.subtract))
                V1(lambda: v.e.tensor_scalar(t1c[:, :], t1c[:, :], 1.0e-20,
                                             None, ALU.add))
                V1(lambda: v.e.reciprocal(dinv[:, :], t1c[:, :]))
                E(v, [lambda sl=sl: v.e.tensor_scalar(
                        comb[:, sl * 16:(sl + 1) * 16], gtmp[:, :],
                        dinv[:, 0:1], None, ALU.mult)],
                  reads=[rt], writes=[R["comb"]])
                yield

            barrier([R["gw_sb"], R["gwt"]], [R["wkg"], R["wvq"]])
            E(g, [lambda: g.e.dma_start(
                    out=wkg_sb[:, :].rearrange("p (kc m) -> p kc m", m=128),
                    in_=I["wkg"].rearrange("(kc p) m -> p kc m", p=128))],
              writes=[R["wkg"]], inc=16, cname="dwa")

            rh = [R["xnf0"], R["xnf1"]]
            for tile in range(32):
                if tile == 16:
                    E(g, [lambda: g.e.dma_start(
                            out=wks_sb[:, :].rearrange("p (kc m) -> p kc m",
                                                       m=128),
                            in_=I["wks"].rearrange("(kc p) m -> p kc m",
                                                   p=128))],
                      writes=[R["wks"]], inc=16, cname="dwa")
                for _ in ln_tile(I["xb"], tile * 128, "dx", xnfm, rh, tile % 2,
                                 None):
                    yield
                yield
                pp, pb = pick_pair()
                psK = pp[:, 0:128]
                E(t, [(lambda kc=kc, psK=psK: t.e.matmul(
                        psK[:, :], wkg_sb[:, kc * 128:(kc + 1) * 128],
                        xnfm[:, kc * 128:(kc + 1) * 128],
                        start=(kc == 0), stop=(kc == 15)))
                      for kc in range(16)],
                  reads=[R["wkg"], rh[0], rh[1]], writes=pb)
                E(s, [lambda psK=psK: s.e.activation(kgraw[:, :],
                                                     psK[:, :], AF.Copy)],
                  reads=pb, writes=[R["kgraw"]])
                pp2, pb2 = pick_pair()
                psS = pp2[:, 0:128]
                E(t, [lambda psS=psS: t.e.matmul(
                        psS[:, :], pswap_sb[:, :], kgraw[:, :],
                        start=True, stop=True)],
                  reads=[R["consts"], R["kgraw"]], writes=pb2)
                E(s, [lambda psS=psS: s.e.activation(kgsw[:, :], psS[:, :],
                                                     AF.Copy)],
                  reads=pb2, writes=[R["kgsw"]])
                E(y, [lambda tile=tile: y.e.dma_start(
                        out=csb[:, :],
                        in_=I["csc"][:, tile * 128:(tile + 1) * 128])],
                  writes=[R["csb"]], inc=16, cname="dcs")
                E(v, [lambda: v.e.tensor_tensor(
                        kgraw[:, :], kgraw[:, :], csb[:, :], ALU.mult)],
                  reads=[R["csb"]], writes=[R["kgraw"]])
                E(y, [lambda tile=tile: y.e.dma_start(
                        out=csb[:, :],
                        in_=I["css"][:, tile * 128:(tile + 1) * 128])],
                  writes=[R["csb"]], inc=16, cname="dcs")
                E(v, [lambda: v.e.tensor_tensor(
                        kgsw[:, :], kgsw[:, :], csb[:, :], ALU.mult)],
                  reads=[R["csb"]], writes=[R["kgsw"]])
                E(v, [lambda: v.e.tensor_tensor(kgsw[:, :], kgsw[:, :],
                                                kgraw[:, :], ALU.add)],
                  reads=[R["kgraw"]], writes=[R["kgsw"]])
                yield
                if tile < 24:
                    E(v, [lambda tile=tile: v.e.tensor_reduce(
                            kgp[:, 2 * tile:2 * tile + 2],
                            kgsw[:, :].rearrange("p (b w) -> p b w", w=64),
                            AX.X, ALU.add)],
                      reads=[R["kgsw"]], writes=[R["kgp"]])
                    E(v, [lambda: v.e.tensor_reduce(
                            pscr.rearrange("p (kc q) -> p kc q", q=2),
                            xnfm[:, :].rearrange("p (kc b w) -> p kc b w",
                                                 b=2, w=64),
                            AX.X, ALU.add)],
                      reads=rh, writes=[R["pscr"]])
                    E(v, [lambda tile=tile: v.e.tensor_copy(
                            pxn[:, :].rearrange("p (kc q) -> p kc q", q=NPOOL)
                            [:, :, 2 * tile:2 * tile + 2],
                            pscr.rearrange("p (kc q) -> p kc q", q=2))],
                      reads=[R["pscr"]], writes=[R["pxn"]])
                else:
                    j = tile - 24
                    off = (1 + j) * 128
                    for gg in range(HKV):
                        E(v, [lambda gg=gg, off=off: v.e.tensor_copy(
                                kaug[gg][32:64, off:off + 128],
                                kgsw[32 * gg:32 * gg + 32, :])],
                          reads=[R["kgsw"]], writes=[KA[gg]])
                    pp3, pb3 = pick_pair()
                    psM = pp3[:, 0:128]
                    E(t, [(lambda kc=kc, psM=psM: t.e.matmul(
                            psM[:, :], wks_sb[:, kc * 128:(kc + 1) * 128],
                            xnfm[:, kc * 128:(kc + 1) * 128],
                            start=(kc == 0), stop=(kc == 15)))
                          for kc in range(16)],
                      reads=[R["wks"], rh[0], rh[1]], writes=pb3)
                    E(s, [(lambda gg=gg, off=off, psM=psM: s.e.activation(
                            kaug[gg][0:32, off:off + 128],
                            psM[32 * gg:32 * gg + 32, :], AF.Copy))
                          for gg in range(HKV)],
                      reads=pb3, writes=KA)
                    yield
                    for qd in range(4):
                        E(g, [lambda qd=qd: g.e.dma_start(
                                out=wvq_sb[:, :].rearrange(
                                    "p (kc m) -> p kc m", m=64),
                                in_=I["wvL"][qd])],
                          writes=[R["wvq"]], inc=16, cname="dwa")
                        pp4, pb4 = pick_pair()
                        psV = pp4[:, 0:64]
                        E(t, [(lambda kc=kc, psV=psV: t.e.matmul(
                                psV[:, :], xnfm[:, kc * 128:(kc + 1) * 128],
                                wvq_sb[:, kc * 64:(kc + 1) * 64],
                                start=(kc == 0), stop=(kc == 15)))
                              for kc in range(16)],
                          reads=[R["wvq"], rh[0], rh[1]], writes=pb4)
                        E(s, [lambda j=j, qd=qd, psV=psV: s.e.activation(
                                v_sb[1 + j][:, qd * 64:(qd + 1) * 64],
                                psV[:, :], AF.Copy)],
                          reads=pb4, writes=[R[f"v{1 + j}"]])
                        if qd % 2 == 1:
                            yield
            pp, pb = pick_pair()
            E(t, [(lambda kc=kc, pp=pp: t.e.matmul(
                    pp[:, 0:NPOOL], wks_sb[:, kc * 128:(kc + 1) * 128],
                    pxn[:, kc * NPOOL:(kc + 1) * NPOOL],
                    start=(kc == 0), stop=(kc == 15))) for kc in range(16)],
              reads=[R["wks"], R["pxn"]], writes=pb)
            E(s, [(lambda gg=gg, pp=pp: s.e.activation(
                    kaug[gg][0:32, 0:NPOOL],
                    pp[32 * gg:32 * gg + 32, 0:NPOOL], AF.Copy,
                    scale=1.0 / 64.0)) for gg in range(HKV)],
              reads=pb, writes=KA)
            yield
            for qd in range(4):
                E(g, [lambda qd=qd: g.e.dma_start(
                        out=wvq_sb[:, :].rearrange("p (kc m) -> p kc m", m=64),
                        in_=I["wvL"][qd])],
                  writes=[R["wvq"]], inc=16, cname="dwa")
                pp4, pb4 = pick_pair()
                E(t, [(lambda kc=kc, pp4=pp4: t.e.matmul(
                        pp4[0:48, 0:64], pxn[:, kc * NPOOL:(kc + 1) * NPOOL],
                        wvq_sb[:, kc * 64:(kc + 1) * 64],
                        start=(kc == 0), stop=(kc == 15)))
                      for kc in range(16)],
                  reads=[R["wvq"], R["pxn"]], writes=pb4)
                E(s, [lambda qd=qd, pp4=pp4: s.e.activation(
                        v_sb[0][0:48, qd * 64:(qd + 1) * 64],
                        pp4[0:48, 0:64], AF.Copy, scale=1.0 / 64.0)],
                  reads=pb4, writes=[R["v0"]])
                if qd % 2 == 1:
                    yield
            for gg in range(HKV):
                E(v, [lambda gg=gg: v.e.tensor_scalar(
                        kaug[gg][32:64, 0:NPOOL], kgp[32 * gg:32 * gg + 32, :],
                        1.0 / 64.0, None, ALU.mult)],
                  reads=[R["kgp"]], writes=[KA[gg]])
            pp, pb = pick_pair()
            E(t, [(lambda c=c, pp=pp: t.e.matmul(
                    pp[0:1, 0:256], ones_col[:, :], v_sb[c][:, :],
                    start=(c == 1), stop=(c == 8))) for c in range(1, 9)],
              reads=[R["consts"]] + [R[f"v{c}"] for c in range(1, 9)],
              writes=pb)
            E(s, [lambda pp=pp: s.e.activation(v_sb[0][64:65, :],
                                               pp[0:1, 0:256],
                                               AF.Copy, scale=1.0 / 1072.0)],
              reads=pb, writes=[R["v0"]])
            yield

            barrier([R["wkg"], R["wks"], R["wvq"], R["xnf0"], R["xnf1"],
                     R["pxn"], R["kgraw"], R["kgsw"], R["csb"], R["kgp"]],
                    [R["wqb"], R["qslot"], R["avac"], R["avst"],
                     R["stats"], R["expT0"], R["mk0"], R["mk1"], R["cso"],
                     R["qgbuf"], R["mpr"]])

            mk_state = {"i": 0}

            def load_mask(kind, sl, c):
                i = mk_state["i"] % 2
                mk_state["i"] += 1
                mb, mrk = mk_r[i], R[f"mk{i}"]
                E(y, [lambda mb=mb, kind=kind, sl=sl, c=c: y.e.dma_start(
                        out=mb[:, :], in_=I[kind][sl, c])],
                  writes=[mrk], inc=16, cname="dmk")
                return mb, mrk

            for sl in range(NSLOT):
                vis = VIS[sl]
                nv = len(vis)
                E(y, [lambda sl=sl: y.e.dma_start(
                        out=cso_sl[:, 0:128],
                        in_=I["cso_c"][:, sl * 128:(sl + 1) * 128]),
                      lambda sl=sl: y.e.dma_start(
                        out=cso_sl[:, 128:256],
                        in_=I["cso_s"][:, sl * 128:(sl + 1) * 128])],
                  writes=[R["cso"]], inc=32, cname="dmk")
                for hf in range(2):
                    for cql in range(2):
                        cq = 2 * hf + cql
                        E(g, [lambda cq=cq: g.e.dma_start(
                                out=wqb[:, :].rearrange(
                                    "p (kc c) -> p kc c", c=128),
                                in_=I["wqsL"][cq])],
                          writes=[R["wqb"]], inc=16, cname="dcs")
                        pp, pb = pick_pair()
                        psQ = pp[:, 0:128]
                        E(t, [(lambda kc=kc, sl=sl, psQ=psQ: t.e.matmul(
                                psQ[:, :], wqb[:, kc * 128:(kc + 1) * 128],
                                xn_own[:, kc * 1024 + sl * 128:
                                       kc * 1024 + sl * 128 + 128],
                                start=(kc == 0), stop=(kc == 15)))
                              for kc in range(16)],
                          reads=[R["wqb"], R["xn_own"]], writes=pb)
                        E(s, [(lambda hl=4 * cql + hh, hh=hh, cq=cq, psQ=psQ:
                               s.e.activation(
                                qslot[0:32, hl * 128:(hl + 1) * 128],
                                psQ[32 * hh:32 * hh + 32, :], AF.Copy,
                                scale=scol[32 * hh:32 * hh + 32, cq:cq + 1]))
                              for hh in range(4)],
                          reads=pb + [R["scol"]], writes=[R["qslot"]])
                        E(g, [lambda cq=cq: g.e.dma_start(
                                out=wqb[:, :].rearrange(
                                    "p (kc c) -> p kc c", c=128),
                                in_=I["wqgL"][cq])],
                          writes=[R["wqb"]], inc=16, cname="dcs")
                        pp2, pb2 = pick_pair()
                        psG = pp2[:, 0:128]
                        E(t, [(lambda kc=kc, sl=sl, psG=psG: t.e.matmul(
                                psG[:, :], wqb[:, kc * 128:(kc + 1) * 128],
                                xn_own[:, kc * 1024 + sl * 128:
                                       kc * 1024 + sl * 128 + 128],
                                start=(kc == 0), stop=(kc == 15)))
                              for kc in range(16)],
                          reads=[R["wqb"], R["xn_own"]], writes=pb2)
                        E(s, [lambda cq=cq, psG=psG: s.e.activation(
                                qg1[:, :], psG[:, :], AF.Copy,
                                scale=scol[:, cq:cq + 1])],
                          reads=pb2 + [R["scol"]], writes=[R["qgbuf"]])
                        pp3, pb3 = pick_pair()
                        psW = pp3[:, 0:128]
                        E(t, [lambda psW=psW: t.e.matmul(
                                psW[:, :], pswap_sb[:, :], qg1[:, :],
                                start=True, stop=True)],
                          reads=[R["consts"], R["qgbuf"]], writes=pb3)
                        E(s, [lambda psW=psW: s.e.activation(
                                qg2[:, :], psW[:, :], AF.Copy)],
                          reads=pb3, writes=[R["qgbuf"]])
                        E(v, [lambda: v.e.tensor_tensor(
                                qg1[:, :], qg1[:, :], cso_sl[:, 0:128],
                                ALU.mult)],
                          reads=[R["cso"]], writes=[R["qgbuf"]])
                        E(v, [lambda: v.e.tensor_tensor(
                                qg2[:, :], qg2[:, :], cso_sl[:, 128:256],
                                ALU.mult)],
                          reads=[R["cso"]], writes=[R["qgbuf"]])
                        E(v, [lambda: v.e.tensor_tensor(qg1[:, :], qg1[:, :],
                                                        qg2[:, :], ALU.add)],
                          writes=[R["qgbuf"]])
                        E(v, [(lambda hl=4 * cql + hh, hh=hh: v.e.tensor_copy(
                                qslot[32:64, hl * 128:(hl + 1) * 128],
                                qg1[32 * hh:32 * hh + 32, :]))
                              for hh in range(4)],
                          reads=[R["qgbuf"]], writes=[R["qslot"]])
                        yield

                    E(v, [lambda: v.e.memset(mcst[:, :], -3.0e38),
                          lambda: v.e.memset(dcst[:, :], 0.0)],
                      writes=[R["stats"]])
                    for ci, c in enumerate(vis):
                        mb, mrk = load_mask("mstd", sl, c)
                        pp, pb = pick_pair()
                        E(t, [(lambda hh=hh, h=8 * hf + hh, c=c, pp=pp:
                               t.e.matmul(
                                pp[:, hh * 128:(hh + 1) * 128],
                                qslot[0:64, hh * 128:(hh + 1) * 128],
                                kaug[h // 4][0:64, c * 128:(c + 1) * 128],
                                start=True, stop=True)) for hh in range(8)],
                          reads=[R["qslot"]] + KA, writes=pb)
                        E(v, [lambda mb=mb, pp=pp: v.e.tensor_tensor(
                                pp.rearrange("p (h k) -> p h k", k=128),
                                pp.rearrange("p (h k) -> p h k", k=128),
                                mb[:, :].rearrange("p (a k) -> p a k", a=1)
                                .broadcast_to([128, 8, 128]), ALU.add)],
                          reads=[mrk], writes=pb)
                        E(v, [lambda c=c, pp=pp: v.e.tensor_reduce(
                                mcst[:, :].rearrange("p (h c) -> p h c", c=9)
                                [:, :, c:c + 1],
                                pp.rearrange("p (h k) -> p h k", k=128),
                                AX.X, ALU.max)],
                          reads=pb, writes=[R["stats"]])
                        E(v, [lambda: v.e.tensor_scalar(
                                negm[:, :], mcst[:, :], -1.0, None, ALU.mult)],
                          reads=[R["stats"]], writes=[R["stats"]])
                        E(s, [(lambda hh=hh, c=c, pp=pp: s.e.activation(
                                expT[:, hh * 128:(hh + 1) * 128],
                                pp[:, hh * 128:(hh + 1) * 128], AF.Exp,
                                bias=negm[:, hh * 9 + c:hh * 9 + c + 1],
                                accum_out=dcst[:, hh * 9 + c:hh * 9 + c + 1]))
                              for hh in range(8)],
                          reads=pb + [R["stats"]],
                          writes=[R["expT0"], R["stats"]])
                        yield

                    E(v, [lambda: v.e.tensor_reduce(
                            mall[:, :],
                            mcst[:, :].rearrange("p (h c) -> p h c", c=9),
                            AX.X, ALU.max)], reads=[R["stats"]],
                      writes=[R["stats"]])
                    E(v, [lambda: v.e.tensor_tensor(
                            edst[:, :].rearrange("p (h c) -> p h c", c=9),
                            mcst[:, :].rearrange("p (h c) -> p h c", c=9),
                            mall[:, :].rearrange("p (h c) -> p h c", c=1)
                            .broadcast_to([128, 8, 9]), ALU.subtract)],
                      reads=[R["stats"]], writes=[R["stats"]])
                    E(s, [lambda: s.e.activation(edst[:, :], edst[:, :],
                                                 AF.Exp)],
                      reads=[R["stats"]], writes=[R["stats"]])
                    E(v, [lambda: v.e.tensor_tensor(edst[:, :], edst[:, :],
                                                    dcst[:, :], ALU.mult)],
                      reads=[R["stats"]], writes=[R["stats"]])
                    E(v, [lambda: v.e.tensor_reduce(
                            dall[:, :],
                            edst[:, :].rearrange("p (h c) -> p h c", c=9),
                            AX.X, ALU.add)], reads=[R["stats"]],
                      writes=[R["stats"]])
                    E(s, [lambda: s.e.activation(dall[:, :], dall[:, :],
                                                 AF.Ln)],
                      reads=[R["stats"]], writes=[R["stats"]])
                    E(v, [lambda: v.e.tensor_tensor(mprime[:, :], mall[:, :],
                                                    dall[:, :], ALU.add)],
                      reads=[R["stats"]], writes=[R["mpr"]])
                    E(v, [lambda: v.e.tensor_copy(mprbf[:, :], mprime[:, :])],
                      reads=[R["mpr"]], writes=[R["mpr"]])
                    pp, pb = pick_pair()
                    E(t, [(lambda hh=hh, pp=pp: t.e.matmul(
                            pp[0:1, hh * 128:(hh + 1) * 128],
                            mprbf[:, hh:hh + 1], negid_sb[:, :],
                            start=True, stop=True)) for hh in range(8)],
                      reads=[R["mpr"], R["consts"]], writes=pb)
                    E(v, [(lambda hh=hh, pp=pp: v.e.tensor_copy(
                            qslot[64:65, hh * 128:(hh + 1) * 128],
                            pp[0:1, hh * 128:(hh + 1) * 128]))
                          for hh in range(8)],
                      reads=pb, writes=[R["qslot"]])
                    if nv > 1:
                        E(v, [lambda: v.e.memset(av_acc[:, :], 0.0)],
                          writes=[R["avac"]])
                    yield

                    for ci, c in enumerate(vis):
                        mb, mrk = load_mask("mtt", sl, c)
                        if sl == 0 and ci == 0:
                            E(y, [lambda: y.e.dma_start(out=qg1[:, :],
                                                        in_=I["degA"]),
                                  lambda: y.e.dma_start(out=qg2[:, :],
                                                        in_=I["degB"])],
                              writes=[R["qgbuf"]], inc=32, cname="dmk")
                        pp, pb = pick_pair()
                        E(t, [(lambda hh=hh, h=8 * hf + hh, c=c, pp=pp:
                               t.e.matmul(
                                pp[:, hh * 128:(hh + 1) * 128],
                                kaug[h // 4][0:65, c * 128:(c + 1) * 128],
                                qslot[0:65, hh * 128:(hh + 1) * 128],
                                start=True, stop=True)) for hh in range(8)],
                          reads=[R["qslot"]] + KA, writes=pb)
                        E(v, [lambda mb=mb, pp=pp: v.e.tensor_tensor(
                                pp.rearrange("p (h k) -> p h k", k=128),
                                pp.rearrange("p (h k) -> p h k", k=128),
                                mb[:, :].rearrange("p (a k) -> p a k", a=1)
                                .broadcast_to([128, 8, 128]), ALU.add)],
                          reads=[mrk], writes=pb)
                        E(s, [lambda pp=pp: s.e.activation(
                                expT[:, :], pp[:, :], AF.Exp)],
                          reads=pb, writes=[R["expT0"]])
                        if sl == 0 and ci == 0:
                            E(v, [lambda: v.e.tensor_tensor(
                                    expT[:, :].rearrange(
                                        "p (h k) -> p h k", k=128),
                                    expT[:, :].rearrange(
                                        "p (h k) -> p h k", k=128),
                                    qg1[:, :].rearrange(
                                        "p (a k) -> p a k", a=1)
                                    .broadcast_to([128, 8, 128]), ALU.mult)],
                              reads=[R["qgbuf"]], writes=[R["expT0"]])
                            E(v, [lambda: v.e.tensor_tensor(
                                    expT[:, :].rearrange(
                                        "p (h k) -> p h k", k=128),
                                    expT[:, :].rearrange(
                                        "p (h k) -> p h k", k=128),
                                    qg2[:, :].rearrange(
                                        "p (a k) -> p a k", a=1)
                                    .broadcast_to([128, 8, 128]), ALU.add)],
                              reads=[R["qgbuf"]], writes=[R["expT0"]])
                        pp2, pb2 = pick_pair()
                        E(t, [(lambda hh=hh, h=8 * hf + hh, c=c, pp2=pp2:
                               t.e.matmul(
                                pp2[0:64, hh * 128:(hh + 1) * 128],
                                v_sb[c][:, (h // 4) * 64:(h // 4) * 64 + 64],
                                expT[:, hh * 128:(hh + 1) * 128],
                                start=True, stop=True)) for hh in range(8)],
                          reads=[R["expT0"], R[f"v{c}"]], writes=pb2)
                        if nv > 1:
                            E(v, [lambda pp2=pp2: v.e.scalar_tensor_tensor(
                                    av_acc[:, :], pp2[0:64, :], 1.0,
                                    av_acc[:, :], ALU.mult, ALU.add)],
                              reads=pb2, writes=[R["avac"]])
                        else:
                            E(s, [(lambda hh=hh, h=8 * hf + hh, pp2=pp2:
                                   s.e.activation(
                                    av_st[(h % 2) * 64:(h % 2) * 64 + 64,
                                          (h // 2) * 128:(h // 2) * 128 + 128],
                                    pp2[0:64, hh * 128:(hh + 1) * 128],
                                    AF.Copy)) for hh in range(8)],
                              reads=pb2, writes=[R["avst"]])
                        yield
                    if nv > 1:
                        E(s, [(lambda hh=hh, h=8 * hf + hh: s.e.activation(
                                av_st[(h % 2) * 64:(h % 2) * 64 + 64,
                                      (h // 2) * 128:(h // 2) * 128 + 128],
                                av_acc[0:64, hh * 128:(hh + 1) * 128],
                                AF.Copy)) for hh in range(8)],
                          reads=[R["avac"]], writes=[R["avst"]])
                E(y, [lambda sl=sl: y.e.dma_start(
                        out=av_dram[sl], in_=av_st[:, :])],
                  reads=[R["avst"]], inc=16, cname="dav")
                yield

        def ffn_chain():
            w1seq = [(e, hcx) for e in range(NXP) for hcx in range(11)]
            w1st = {"i": 0}

            def issue_w1():
                i = w1st["i"]
                if i >= len(w1seq):
                    return
                w1st["i"] = i + 1
                e, hcx = w1seq[i]
                slot = i % 3
                E(g, [lambda e=e, hcx=hcx, slot=slot: g.e.dma_start(
                        out=w1r[slot][:, :].rearrange("p (kc c) -> p kc c",
                                                      c=128),
                        in_=I["w1L"][e, hcx])],
                  writes=[R[f"w1r{slot}"]], inc=16, cname="dwf")

            def load_w2(e):
                E(g, [lambda e=e: g.e.dma_start(
                        out=w2_sb[:, :].rearrange("p (hc n) -> p hc n", n=D),
                        in_=I["w2A"][e].rearrange("(hc p) n -> p hc n",
                                                  p=128))],
                  writes=[R["w2"]], inc=16, cname="dwf")

            for _ in range(3):
                issue_w1()
            load_w2(0)

            for e in range(NXP):
                ce = None if e == 0 else e - 1
                for hcx in range(11):
                    gi = e * 11 + hcx
                    slot = gi % 3
                    rw1 = R[f"w1r{slot}"]
                    pk = [R[b] for b in PAIRB[hcx % 2]]
                    pp = ps1[hcx % 2]
                    E(t, [(lambda kc=kc, th=th, slot=slot, pp=pp:
                           t.e.matmul(
                            pp[:, th * 512:(th + 1) * 512],
                            w1r[slot][:, kc * 128:(kc + 1) * 128],
                            xn_own[:, kc * 1024 + th * 512:
                                   kc * 1024 + (th + 1) * 512],
                            start=(kc == 0), stop=(kc == 15)))
                          for kc in range(16) for th in range(2)],
                      reads=[rw1, R["xn_own"]], writes=pk)
                    issue_w1()
                    E(s, [lambda hcx=hcx, pp=pp: s.e.activation(
                            silu_sb[:, hcx * 1024:(hcx + 1) * 1024],
                            pp[:, :], AF.Silu)],
                      reads=pk, writes=[R["silu"]])
                    yield
                for tc in range(NSLOT):
                    for half in range(2):
                        bkh = [R[b] for b in MM2B[half]]
                        ocs = (0, 1) if half == 0 else (2, 3)
                        E(t, [(lambda hcx=hcx, oc=oc, tc=tc: t.e.matmul(
                                ps2[:, oc * 512:(oc + 1) * 512],
                                silu_sb[:, hcx * 1024 + tc * 128:
                                        hcx * 1024 + tc * 128 + 128],
                                w2_sb[:, hcx * D + oc * 512:
                                      hcx * D + (oc + 1) * 512],
                                start=(hcx == 0), stop=(hcx == 10)))
                              for hcx in range(11) for oc in ocs],
                          reads=[R["silu"], R["w2"]], writes=bkh)
                        if ce is None:
                            E(v, [lambda tc=tc, half=half:
                                  v.e.scalar_tensor_tensor(
                                    y_acc[:, tc * D + half * 1024:
                                          tc * D + half * 1024 + 1024],
                                    ps2[:, half * 1024:(half + 1) * 1024],
                                    1.0,
                                    y_acc[:, tc * D + half * 1024:
                                          tc * D + half * 1024 + 1024],
                                    ALU.mult, ALU.add)],
                              reads=bkh, writes=[R[f"yac{tc}"]])
                        else:
                            E(v, [lambda tc=tc, ce=ce, half=half:
                                  v.e.scalar_tensor_tensor(
                                    y_acc[:, tc * D + half * 1024:
                                          tc * D + half * 1024 + 1024],
                                    ps2[:, half * 1024:(half + 1) * 1024],
                                    comb[:, tc * 16 + ce:tc * 16 + ce + 1],
                                    y_acc[:, tc * D + half * 1024:
                                          tc * D + half * 1024 + 1024],
                                    ALU.mult, ALU.add)],
                              reads=bkh + [R["comb"]],
                              writes=[R[f"yac{tc}"]])
                    yield
                if e + 1 < NXP:
                    load_w2(e + 1)

        side = side_chain()
        ffn = ffn_chain()
        n_slices = NXP * 19
        n_side = 336
        credit = 0.0
        side_done = False
        while True:
            try:
                next(ffn)
            except StopIteration:
                break
            credit += n_side / float(n_slices)
            while not side_done and credit >= 1.0:
                credit -= 1.0
                try:
                    next(side)
                except StopIteration:
                    side_done = True
        while not side_done:
            try:
                next(side)
            except StopIteration:
                side_done = True

        barrier([R["w2"], R["silu"]], [R["wo"], R["avt0"], R["avt1"]])
        E(g, [lambda: g.e.dma_start(
                out=wo_sb[:, :].rearrange("p (kc m) -> p kc m", m=D),
                in_=I["wo"].rearrange("(kc p) m -> p kc m", p=128))],
          writes=[R["wo"]], inc=16, cname="dwf")
        for sl in range(NSLOT):
            ast = avt[sl % 2]
            rast = R[f"avt{sl % 2}"]
            E(y, [lambda sl=sl, ast=ast: y.e.dma_start(
                    out=ast[:, :], in_=av_dram[sl])],
              writes=[rast], inc=16, cname="dav")
            for oc in range(4):
                E(t, [(lambda hd=hd, oc=oc, ast=ast: t.e.matmul(
                        ps2[:, oc * 512:(oc + 1) * 512],
                        ast[:, hd * 128:(hd + 1) * 128],
                        wo_sb[:, hd * D + oc * 512: hd * D + (oc + 1) * 512],
                        start=(hd == 0), stop=(hd == 7))) for hd in range(8)],
                  reads=[rast, R["wo"]], writes=[R[f"b{4 + oc}"]])
            E(v, [lambda sl=sl: v.e.tensor_tensor(
                    y_acc[:, sl * D:(sl + 1) * D],
                    y_acc[:, sl * D:(sl + 1) * D], ps2[:, 0:2048],
                    ALU.add)],
              reads=[R[f"b{4 + oc}"] for oc in range(4)],
              writes=[R[f"yac{sl}"]])
            E(y, [lambda sl=sl: y.e.dma_start(
                    out=y_out[sl * 128:(sl + 1) * 128, :],
                    in_=y_acc[:, sl * D:(sl + 1) * D])],
              reads=[R[f"yac{sl}"]], inc=16, cname="dou")
        io.wait(y, "dou", io.n("dou"))
        return io

    counts = {}
    for nm, eng_attr in [("t", "tensor"), ("v", "vector"), ("s", "scalar"),
                         ("g", "gpsimd"), ("y", "sync")]:
        def body(eng, nm=nm):
            io = IO(sems, nm, eng)
            program(io)
            counts[nm] = dict(io.cnt)
        getattr(blk, eng_attr)(body)
    base = counts["t"]
    for nm in ["v", "s", "g", "y"]:
        assert counts[nm] == base, (nm, counts[nm], base)

    ctx.close()
    return nc


def host_constants(q):
    out = {}
    inv = 1.0 / (10000.0 ** (np.arange(0, 32, 2, dtype=np.float64) / 32.0))
    pos = np.arange(T, dtype=np.float64)
    ang = pos[None, :] * inv[:, None]
    cos = np.cos(ang).astype(np.float32)
    sin = np.sin(ang).astype(np.float32)
    ctab = np.zeros((128, T), np.float32)
    stab = np.zeros((128, T), np.float32)
    for hh in range(4):
        ctab[hh * 32:hh * 32 + 16] = cos
        ctab[hh * 32 + 16:hh * 32 + 32] = cos
        stab[hh * 32:hh * 32 + 16] = -sin
        stab[hh * 32 + 16:hh * 32 + 32] = sin
    out["csc"] = ctab.astype(ml_dtypes.bfloat16)
    out["css"] = stab.astype(ml_dtypes.bfloat16)
    own_tiles = [q + 4 * s for s in range(NSLOT)]
    own_cols = np.concatenate([np.arange(t0 * 128, t0 * 128 + 128)
                               for t0 in own_tiles])
    out["cso_c"] = np.ascontiguousarray(ctab[:, own_cols]).astype(
        ml_dtypes.bfloat16)
    out["cso_s"] = np.ascontiguousarray(stab[:, own_cols]).astype(
        ml_dtypes.bfloat16)
    out["negid"] = (-np.eye(128, dtype=np.float32)).astype(ml_dtypes.bfloat16)
    psw = np.zeros((128, 128), np.float32)
    for hh in range(4):
        for dd in range(16):
            psw[hh * 32 + dd, hh * 32 + 16 + dd] = 1.0
            psw[hh * 32 + 16 + dd, hh * 32 + dd] = 1.0
    out["pswap"] = psw.astype(ml_dtypes.bfloat16)

    kpos = np.zeros(9 * 128, np.int64)
    kreal = np.zeros(9 * 128, bool)
    kpos[0:48] = 64 * np.arange(48) + 63
    kreal[0:48] = True
    for j in range(8):
        kpos[128 * (1 + j):128 * (2 + j)] = 3072 + 128 * j + np.arange(128)
        kreal[128 * (1 + j):128 * (2 + j)] = True
    mstd = np.zeros((NSLOT, 9, 128, 128), np.float32)
    mtt = np.zeros((NSLOT, 9, 128, 128), np.float32)
    for sidx in range(NSLOT):
        t0 = own_tiles[sidx]
        qpos = t0 * 128 + np.arange(128)
        for c in range(9):
            kp = kpos[c * 128:(c + 1) * 128]
            kr = kreal[c * 128:(c + 1) * 128]
            visible = (kp[None, :] <= qpos[:, None]) & kr[None, :]
            m = np.where(visible, 0.0, NEG).astype(np.float32)
            mstd[sidx, c] = m
            mtt[sidx, c] = m.T
    out["mstd"] = mstd.astype(ml_dtypes.bfloat16)
    out["mtt"] = mtt.astype(ml_dtypes.bfloat16)

    degA = np.ones((128, 128), np.float32)
    degB = np.zeros((128, 128), np.float32)
    degA[64, :] = 0.0
    if q == 0:
        degA[:, 0:63] = 0.0
        degB[0:48, 0:63] = 1.0 / 1072.0
        degB[64, 0:63] = 1.0
    out["degA"] = degA.astype(ml_dtypes.bfloat16)
    out["degB"] = degB.astype(ml_dtypes.bfloat16)
    return out, own_cols


def make_in_maps(x, weights, lsx):
    in_maps = []
    own_cols_all = []
    for c in range(8):
        b, q = c // 4, c % 4
        consts, own_cols = host_constants(q)
        own_cols_all.append(own_cols)
        m = dict(weights)
        m["lsx"] = lsx
        m.update(consts)
        m["xb"] = x[b]
        m["x_own"] = np.ascontiguousarray(x[b][own_cols])
        in_maps.append(m)
    return in_maps, own_cols_all


_CACHE = {}


def kernel(L0_attn_x, L0_attn_Wq_sem, L0_attn_Wk_sem, L0_attn_Wq_geo,
           L0_attn_Wk_geo, L0_attn_Wv, L0_attn_Wo, L0_attn_logit_scale,
           L0_ffn_gate_w, L0_ffn_w1, L0_ffn_w2, L0_ffn_sw1, L0_ffn_sw2):
    x = np.ascontiguousarray(np.asarray(L0_attn_x, dtype=np.float32))
    ls = np.asarray(L0_attn_logit_scale, dtype=np.float32)
    lsx = np.zeros((128, 4), np.float32)
    for cq in range(4):
        for hh in range(4):
            lsx[hh * 32:(hh + 1) * 32, cq] = ls[4 * cq + hh]

    if "nc" not in _CACHE:
        _CACHE["nc"] = build_program()
    nc = _CACHE["nc"]

    f32c = lambda a: np.ascontiguousarray(np.asarray(a, np.float32))
    w1 = np.concatenate([f32c(L0_ffn_sw1), f32c(L0_ffn_w1)], axis=0)
    w1L = np.ascontiguousarray(
        w1.reshape(NXP, 16, 128, 11, 128).transpose(0, 3, 2, 1, 4))
    w2A = np.ascontiguousarray(
        np.concatenate([f32c(L0_ffn_sw2), f32c(L0_ffn_w2)], axis=0))
    wqsL = np.ascontiguousarray(
        f32c(L0_attn_Wq_sem).reshape(16, 128, 4, 128).transpose(2, 1, 0, 3))
    wqgL = np.ascontiguousarray(
        f32c(L0_attn_Wq_geo).reshape(16, 128, 4, 128).transpose(2, 1, 0, 3))
    wvL = np.ascontiguousarray(
        f32c(L0_attn_Wv).reshape(16, 128, 4, 64).transpose(2, 1, 0, 3))
    weights = {"wqsL": wqsL, "wks": f32c(L0_attn_Wk_sem),
               "wqgL": wqgL, "wkg": f32c(L0_attn_Wk_geo),
               "wvL": wvL, "wo": f32c(L0_attn_Wo),
               "gw": f32c(L0_ffn_gate_w), "w1L": w1L, "w2A": w2A}
    in_maps, own_cols_all = make_in_maps(x, weights, lsx)
    res = run_bass_kernel_spmd(nc, in_maps, core_ids=list(range(8)))
    y = np.zeros((B, T, D), np.float32)
    for c in range(8):
        b = c // 4
        y[b][own_cols_all[c]] = res.results[c]["y"]
    return y
